# version 18
# speedup vs baseline: 1.0977x; 1.0205x over previous
import numpy as np
import ml_dtypes
import concourse.bass as bass
import concourse.mybir as mybir
from concourse.bass_utils import run_bass_kernel_spmd
from contextlib import ExitStack

F32 = mybir.dt.float32
BF16 = mybir.dt.bfloat16
AF = mybir.ActivationFunctionType
ALU = mybir.AluOpType
AX = mybir.AxisListType.X

NCORES = 8
BPC = 2
N = 2048
D = 1024
HID = 4096
NH = 16
DH = 64
NT = N // 128
DC = D // 128
HC = HID // 128
RC = 12582912.0
NEG = -30000.0
GAP = 60.0
EPS_LN = 1e-5
EPS_RMS = 1e-6
EPS_Q = 1e-5
ISQ_DH = 0.125

ENGINES = ("sync", "scalar", "vector", "tensor", "gpsimd")


def ts128(i):
    return slice(128 * i, 128 * (i + 1))


NDMASEM = 8


class Prog:

    DMA_QUEUES = ("sync", "gpsimd")

    def __init__(self, nc):
        self.nc = nc
        self.items = {e: [] for e in ENGINES}
        self.count = {}
        self.dma_idx = {q: 0 for q in self.DMA_QUEUES}
        self.waited = {}
        self.sems = {}
        self.sem_names = []
        for e in ENGINES:
            self.sem_names.append(e)
            self.count[e] = 0
        for q in self.DMA_QUEUES:
            for k in range(NDMASEM):
                nm = f"{q}_d{k}"
                self.sem_names.append(nm)
                self.count[nm] = 0

    def init_sems(self, ctx):
        for nm in self.sem_names:
            self.sems[nm] = ctx.enter_context(self.nc.semaphore(f"sem_{nm}"))

    def _wait(self, eng, tok, allow_self=False):
        sem, val = tok
        if sem == eng and not allow_self:
            return
        key = (eng, sem)
        if self.waited.get(key, 0) >= val:
            return
        self.waited[key] = val
        self.items[eng].append(("wait", sem, val))

    def op(self, eng, fn, deps=(), dma=False, self_dep=False, drain=False):
        if dma:
            k = self.dma_idx[eng] % NDMASEM
            self.dma_idx[eng] += 1
            nm = f"{eng}_d{k}"
            if self.count[nm] > 0:
                self.items[eng].append(("wait", nm, self.count[nm]))
            for tok in deps:
                if tok is not None:
                    self._wait(eng, tok, allow_self=True)
            self.count[nm] += 16
            self.items[eng].append(("op", fn, 16, nm))
            return (nm, self.count[nm])
        for tok in deps:
            if tok is not None:
                self._wait(eng, tok, allow_self=True)
        self.count[eng] += 1
        self.items[eng].append(("op", fn, 1, eng))
        if drain:
            self.count[eng] += 1
            self.items[eng].append(("op", lambda h: h.drain(), 1, eng))
        return (eng, self.count[eng])

    def wait(self, eng, *toks, allow_self=False):
        for tok in toks:
            if tok is not None:
                self._wait(eng, tok, allow_self=allow_self)

    def token(self, eng):
        return (eng, self.count[eng])

    def qtokens(self, queue):
        return [(f"{queue}_d{k}", self.count[f"{queue}_d{k}"]) for k in range(NDMASEM)
                if self.count[f"{queue}_d{k}"] > 0]

    def fence(self):
        toks = [(nm, self.count[nm]) for nm in self.sem_names if self.count[nm] > 0]
        for e in ENGINES:
            for tok in toks:
                self._wait(e, tok, allow_self=False)

    def emit(self, block):
        for eng in ENGINES:
            items = self.items[eng]
            sems = self.sems

            def make_body(eng, items):
                def body(h):
                    for it in items:
                        if it[0] == "op":
                            _, fn, inc, nm = it
                            fn(h).then_inc(sems[nm], inc)
                        else:
                            _, snm, val = it
                            h.wait_ge(sems[snm], val)
                return body

            getattr(block, eng)(make_body(eng, items))


class Builder:
    def __init__(self, debug_names=(), stop_after=None):
        self.debug_names = tuple(debug_names)
        self.stop_after = stop_after
        self.nc = bass.Bass("TRN2", target_bir_lowering=False)
        self._uid = 0
        _sb, _ps = self.nc.sbuf_tensor, self.nc.psum_tensor

        def sb(name, *a, **k):
            self._uid += 1
            return _sb(f"{name}_u{self._uid}", *a, **k)

        def ps(name, *a, **k):
            self._uid += 1
            return _ps(f"{name}_u{self._uid}", *a, **k)

        self.nc.sbuf_tensor = sb
        self.nc.psum_tensor = ps
        self.P = Prog(self.nc)
        self.dbg = {}
        self.final_store_toks = []

    def dma(self, queue, out, in_, deps=(), slow=False):
        if slow:
            return self.P.op(queue, lambda h: h.dma_start(
                out=out, in_=in_, allow_slow_non_contiguous=True), deps=deps, dma=True)
        return self.P.op(queue, lambda h: h.dma_start(out=out, in_=in_), deps=deps, dma=True)

    def dbg_out(self, name, shape, dt):
        if name in self.debug_names and name not in self.dbg:
            self.dbg[name] = self.nc.dram_tensor("dbg_" + name, list(shape), dt,
                                                 kind="ExternalOutput").ap()
        return self.dbg.get(name)

    def dump(self, name, sbuf_ap, shape, dt, deps):
        ap = self.dbg_out(name, shape, dt)
        if ap is None:
            return None
        t = self.dma("sync", ap[:, :], sbuf_ap, deps=deps)
        self.final_store_toks.append(t)
        return t

    def build(self):
        nc, P = self.nc, self.P
        self.x_in = nc.dram_tensor("x", [BPC, N, D], F32, kind="ExternalInput").ap()
        for nm in ("wq", "wk", "wv", "wo"):
            setattr(self, nm + "_in", nc.dram_tensor(nm, [D, D], F32, kind="ExternalInput").ap())
        for nm, sh in (("bq", [D]), ("bk", [D]), ("bv", [D]), ("bo", [D]),
                       ("w1", [HID, D]), ("b1", [HID]), ("w2", [D, HID]), ("b2", [D])):
            setattr(self, nm + "_in", nc.dram_tensor(nm, sh, F32, kind="ExternalInput").ap())
        self.ident_in = nc.dram_tensor("c_ident", [128, 128], BF16, kind="ExternalInput").ap()
        self.identf_in = nc.dram_tensor("c_identf", [128, 128], F32, kind="ExternalInput").ap()
        self.triT_in = nc.dram_tensor("c_triT", [128, 128], BF16, kind="ExternalInput").ap()
        self.tri01_in = nc.dram_tensor("c_tri01", [128, 128], BF16, kind="ExternalInput").ap()
        self.ew_in = nc.dram_tensor("c_ew", [128, 896], BF16, kind="ExternalInput").ap()
        self.out_hbm = nc.dram_tensor("out", [BPC, N, D], F32, kind="ExternalOutput").ap()
        self.wqT_s = nc.dram_tensor("wqT_s", [D, D], BF16, kind="Internal").ap()
        self.wkT_s = nc.dram_tensor("wkT_s", [D, D], BF16, kind="Internal").ap()
        self.wvT_s = nc.dram_tensor("wvT_s", [D, D], BF16, kind="Internal").ap()
        self.woT_s = nc.dram_tensor("woT_s", [D, D], BF16, kind="Internal").ap()
        self.w1qT_s = nc.dram_tensor("w1qT_s", [D, HID], BF16, kind="Internal").ap()
        self.w2qT_s = nc.dram_tensor("w2qT_s", [HID, D], BF16, kind="Internal").ap()
        self.xqT_s = nc.dram_tensor("xqT_s", [BPC, D, N], BF16, kind="Internal").ap()
        self.s_s = nc.dram_tensor("s_s", [BPC, N, D], BF16, kind="Internal").ap()
        self.hq_t_s = [nc.dram_tensor(f"hqt_s{t}", [128, HID], BF16, kind="Internal").ap()
                       for t in range(NT)]

        with ExitStack() as top:
            P.init_sems(top)
            e = top.enter_context
            self.ident = e(nc.sbuf_tensor("ident", [128, 128], BF16))
            self.identf = e(nc.sbuf_tensor("identf", [128, 128], F32))
            self.triT = e(nc.sbuf_tensor("triT", [128, 128], BF16))
            self.tri01 = e(nc.sbuf_tensor("tri01", [128, 128], BF16))
            self.ew = e(nc.sbuf_tensor("ew", [128, 896], BF16))
            self.bqs = e(nc.sbuf_tensor("bqs", [128, DC], F32))
            self.bks = e(nc.sbuf_tensor("bks", [128, DC], F32))
            self.bvrow = e(nc.sbuf_tensor("bvrow", [1, D], BF16))
            self.borow = e(nc.sbuf_tensor("borow", [1, D], BF16))
            self.b2row = e(nc.sbuf_tensor("b2row", [1, D], BF16))
            self.onesrow = e(nc.sbuf_tensor("onesrow", [1, 128], BF16))
            self.onesrowf = e(nc.sbuf_tensor("onesrowf", [1, 128], F32))
            self.rcpos = e(nc.sbuf_tensor("rcpos", [128, 1], F32))
            self.rcneg = e(nc.sbuf_tensor("rcneg", [128, 1], F32))
            self.sw1_bc = e(nc.sbuf_tensor("sw1_bc", [128, 1], F32))
            self.sw2_bc = e(nc.sbuf_tensor("sw2_bc", [128, 1], F32))
            self.sw1inv_bc = e(nc.sbuf_tensor("sw1inv_bc", [128, 1], F32))
            self.sw2inv_bc = e(nc.sbuf_tensor("sw2inv_bc", [128, 1], F32))
            self.c1v = e(nc.sbuf_tensor("c1v", [128, BPC * NT], F32))
            self.sxw_row = e(nc.sbuf_tensor("sxw_row", [1, BPC * N], BF16))

            self.load_constants()
            self.phase_w0(top)
            stop = False
            if self.stop_after != "w0":
                for b in range(BPC):
                    with ExitStack() as bscope:
                        be = bscope.enter_context
                        self.QT = [be(nc.sbuf_tensor(f"QT{i}", [128, N], BF16)) for i in range(DC)]
                        self.KT = [be(nc.sbuf_tensor(f"KT{i}", [128, N], BF16)) for i in range(DC)]
                        self.V = [be(nc.sbuf_tensor(f"Vt{t}", [128, D], BF16)) for t in range(NT)]
                        self.AO = ([be(nc.sbuf_tensor(f"AO{t}", [128, D], BF16)) for t in range(NT)]
                                   if self.stop_after != "p1" else None)
                        P.fence()
                        self.phase_p1(b)
                        if self.stop_after == "p1":
                            stop = True
                        if not stop:
                            P.fence()
                            self.phase_p2(b)
                            if self.stop_after == "p2":
                                stop = True
                        if not stop:
                            P.fence()
                            self.phase_p2c(b)
                            if self.stop_after == "p2c":
                                stop = True
                    P.fence()
                    if not stop:
                        self.phase_p3(b)
                        if self.stop_after == "p3a":
                            stop = True
                    if stop:
                        break
            self.finish()
        return self.nc

    def finish(self):
        P = self.P
        toks = [(nm, P.count[nm]) for nm in P.sem_names if P.count[nm] > 0 and nm != "sync"]
        for tok in toks:
            P._wait("sync", tok, allow_self=True)
        with self.nc.Block() as block:
            P.emit(block)

    def load_constants(self):
        P = self.P
        self.t_ident = self.dma("sync", self.ident[:], self.ident_in[:, :])
        self.dma("sync", self.identf[:], self.identf_in[:, :])
        self.dma("sync", self.triT[:], self.triT_in[:, :])
        self.dma("sync", self.tri01[:], self.tri01_in[:, :])
        self.dma("sync", self.ew[:], self.ew_in[:, :])
        self.dma("sync", self.bqs[:], self.bq_in.rearrange("(c p) -> p c", p=128), slow=True)
        self.dma("sync", self.bks[:], self.bk_in.rearrange("(c p) -> p c", p=128), slow=True)
        self.dma("gpsimd", self.bvrow[:], self.bv_in[None, :])
        self.dma("gpsimd", self.borow[:], self.bo_in[None, :])
        self.dma("gpsimd", self.b2row[:], self.b2_in[None, :])
        self.t_bqscale = P.op("vector", lambda v: v.tensor_scalar(
            out=self.bqs[:], in0=self.bqs[:], scalar1=ISQ_DH, scalar2=None, op0=ALU.mult),
            deps=P.qtokens("sync"))
        self.t_const_s = P.qtokens("sync")
        self.t_const_g = P.qtokens("gpsimd")
        self.t_ones = P.op("vector", lambda v: v.memset(self.onesrow[:], 1.0))
        self.t_onesf = P.op("vector", lambda v: v.memset(self.onesrowf[:], 1.0))
        self.t_rcp = P.op("vector", lambda v: v.memset(self.rcpos[:], RC))
        self.t_rcn = P.op("vector", lambda v: v.memset(self.rcneg[:], -RC))

    def phase_w0(self, top):
        nc, P = self.nc, self.P
        with ExitStack() as w0:
            we = w0.enter_context
            wnat = [we(nc.sbuf_tensor(f"w0nat{i}", [128, HID], F32)) for i in range(2)]
            wi32 = we(nc.sbuf_tensor("w0i32", [128, HID], mybir.dt.int32))
            wi32_free = [None]
            wbf = [we(nc.sbuf_tensor(f"w0bf{i}", [128, D], BF16)) for i in range(8)]
            wbf2 = [we(nc.sbuf_tensor(f"w0bf2_{i}", [128, HID], BF16)) for i in range(2)]
            wstg = [we(nc.sbuf_tensor(f"w0stg{i}", [128, 1024], BF16)) for i in range(2)]
            wacc = we(nc.sbuf_tensor("w0acc", [128, 1], F32))
            wacc2 = we(nc.sbuf_tensor("w0acc2", [128, 1], F32))
            wacc3 = we(nc.sbuf_tensor("w0acc3", [128, 1], F32))
            wrow = we(nc.sbuf_tensor("w0row", [1, 128], F32))
            wtot = we(nc.sbuf_tensor("w0tot", [1, 1], F32))
            pst = [we(nc.psum_tensor(f"w0pst{i}", [128, 1024], BF16)) for i in range(4)]
            psr = we(nc.psum_tensor("w0psr", [1, 512], F32))

            P.wait("tensor", self.t_ident)
            wstg_dma = [None, None]

            def transpose_group(tiles, ncols, dst_dram, col0, dep_ready, scale=1.0):
                nj = len(tiles)
                for i in range(ncols // 128):
                    sp = i % 2
                    evs = []
                    for j0 in range(0, nj, 4):
                        nk = min(4, nj - j0)
                        tq_ = None
                        for k in range(nk):
                            jj = j0 + k
                            tq_ = P.op("tensor", lambda t_, src=tiles[jj], i=i, k=k: t_.transpose(
                                pst[k][:, 0:128], src[:, ts128(i)], self.ident[:]),
                                deps=dep_ready + [P.token("scalar")],
                                drain=(k == nk - 1))
                        for k in range(nk):
                            jj = j0 + k
                            tv = P.op("scalar", lambda sc, jj=jj, sp=sp, k=k, scale=scale: sc.activation(
                                out=wstg[sp][:, ts128(jj)], in_=pst[k][:, 0:128],
                                func=AF.Identity, scale=scale),
                                deps=[tq_, wstg_dma[sp]])
                            evs.append(tv)
                    wstg_dma[sp] = self.dma(
                        "sync", dst_dram[ts128(i), col0:col0 + 128 * nj],
                        wstg[sp][:, 0:128 * nj], deps=evs)

            for w_in_ap, w_dst, wscale in (
                    (self.wq_in, self.wqT_s, ISQ_DH), (self.wk_in, self.wkT_s, 1.0),
                    (self.wv_in, self.wvT_s, 1.0), (self.wo_in, self.woT_s, 1.0)):
                lt = []
                for j in range(DC):
                    lt.append(self.dma("gpsimd", wbf[j][:, 0:D], w_in_ap[ts128(j), :],
                                       deps=[P.token("tensor")]))
                transpose_group([wbf[j] for j in range(DC)], D, w_dst, 0, lt, scale=wscale)

            tz = P.op("vector", lambda v: v.memset(wacc[:], 0.0))
            last = tz
            rd_prev = [None, None]
            for t in range(HID // 128):
                par = t % 2
                tl = self.dma("sync", wnat[par][:, 0:D], self.w1_in[ts128(t), :],
                              deps=[rd_prev[par]])
                tr = P.op("vector", lambda v, par=par: v.tensor_reduce(
                    out=wacc2[:], in_=wnat[par][:, 0:D], axis=AX, op=ALU.add,
                    apply_absolute_value=True), deps=[tl, last])
                rd_prev[par] = tr
                last = P.op("vector", lambda v: v.tensor_add(
                    out=wacc[:], in0=wacc[:], in1=wacc2[:]), deps=[tr], self_dep=True)
            t_w1stat = last
            tz2 = P.op("vector", lambda v: v.memset(wacc2[:], 0.0), deps=[t_w1stat], self_dep=True)
            last = tz2
            rd_prev = [None, None]
            for t in range(D // 128):
                par = t % 2
                tl = self.dma("sync", wnat[par][:, 0:HID], self.w2_in[ts128(t), :],
                              deps=[rd_prev[par]])
                tr = P.op("vector", lambda v, par=par: v.tensor_reduce(
                    out=wacc3[:], in_=wnat[par][:, 0:HID], axis=AX, op=ALU.add,
                    apply_absolute_value=True), deps=[tl, last])
                rd_prev[par] = tr
                last = P.op("vector", lambda v: v.tensor_add(
                    out=wacc2[:], in0=wacc2[:], in1=wacc3[:]), deps=[tr], self_dep=True)
            t_w2stat = last

            psbc = w0.enter_context(nc.psum_tensor("w0psbc", [128, 512], F32))

            def w_scale(acc, nelem, dst_inv, dst_scale, deps):
                tt = P.op("tensor", lambda t: t.transpose(psr[:, 0:128], acc[:], self.identf[:]),
                          deps=deps + [P.token("scalar")], drain=True)
                tv = P.op("scalar", lambda sc: sc.copy(out=wrow[:], in_=psr[:, 0:128]), deps=[tt])
                tr = P.op("vector", lambda v: v.tensor_reduce(
                    out=wtot[:], in_=wrow[:], axis=AX, op=ALU.add), deps=[tv])
                tm = P.op("vector", lambda v: v.tensor_scalar(
                    out=wtot[:], in0=wtot[:], scalar1=1.0 / nelem, scalar2=EPS_Q,
                    op0=ALU.mult, op1=ALU.max), deps=[tr], self_dep=True)
                tb = P.op("tensor", lambda t: t.matmul(
                    psbc[:, 0:1], self.onesrowf[:, :], wtot[:, :], start=True, stop=True),
                    deps=[tm, self.t_onesf, P.token("scalar")], drain=True)
                tc = P.op("scalar", lambda sc: sc.copy(out=dst_inv[:], in_=psbc[:, 0:1]), deps=[tb])
                return P.op("vector", lambda v: v.reciprocal(
                    out=dst_scale[:], in_=dst_inv[:]), deps=[tc])

            t_sw1 = w_scale(wacc, HID * D, self.sw1inv_bc, self.sw1_bc, [t_w1stat])
            t_sw2 = w_scale(wacc2, D * HID, self.sw2inv_bc, self.sw2_bc, [t_w2stat])

            def quant_rows(w_in_ap, nrows, ncols, scale_bc, dst_dram, dep_scale,
                           group, bufs):
                for g0 in range(0, nrows // 128, group):
                    qt = []
                    for jj in range(group):
                        t = g0 + jj
                        par = t % 2
                        tl = self.dma("sync", wnat[par][:, 0:ncols], w_in_ap[ts128(t), :],
                                      deps=[P.token("vector")])
                        t1 = P.op("vector", lambda v, par=par, ncols=ncols: v.tensor_scalar(
                            out=wi32[:, 0:ncols], in0=wnat[par][:, 0:ncols],
                            scalar1=scale_bc[:], scalar2=None, op0=ALU.mult),
                            deps=[tl, dep_scale, wi32_free[0]], self_dep=True)
                        t2 = P.op("vector", lambda v, par=par, ncols=ncols: v.tensor_copy(
                            out=wnat[par][:, 0:ncols], in_=wi32[:, 0:ncols]),
                            deps=[t1], self_dep=True)
                        wi32_free[0] = t2
                        t3 = P.op("vector", lambda v, jj=jj, par=par, ncols=ncols, bufs=bufs: v.tensor_scalar(
                            out=bufs[jj][:, 0:ncols], in0=wnat[par][:, 0:ncols],
                            scalar1=1.0, scalar2=-1.0, op0=ALU.min, op1=ALU.max),
                            deps=[t2, P.token("tensor")], self_dep=True)
                        qt.append(t3)
                    transpose_group([bufs[jj] for jj in range(group)], ncols,
                                    dst_dram, 128 * g0, qt)

            quant_rows(self.w1_in, HID, D, self.sw1_bc, self.w1qT_s, t_sw1, 8, wbf)
            quant_rows(self.w2_in, D, HID, self.sw2_bc, self.w2qT_s, t_sw2, 2, wbf2)

            if "wvT" in self.debug_names:
                ap = self.dbg_out("wvT", (128, D), BF16)
                tl = self.dma("sync", wbf[0][:, 0:D], self.wvT_s[ts128(1), :],
                              deps=P.qtokens("sync") + [P.token("tensor")])
                self.final_store_toks.append(
                    self.dma("sync", ap[:, :], wbf[0][:, 0:D], deps=[tl]))
            if "w1qT" in self.debug_names:
                ap = self.dbg_out("w1qT", (128, HID), BF16)
                tl = self.dma("sync", wbf2[0][:, 0:HID], self.w1qT_s[ts128(0), :],
                              deps=P.qtokens("sync") + [P.token("tensor")])
                self.final_store_toks.append(
                    self.dma("sync", ap[:, :], wbf2[0][:, 0:HID], deps=[tl]))
            if "sws" in self.debug_names:
                ap = self.dbg_out("sws", (128, 4), F32)
                stg = we(nc.sbuf_tensor("w0sws", [128, 4], F32))
                P.op("vector", lambda v: v.tensor_copy(out=stg[:, 0:1], in_=self.sw1_bc[:]),
                     deps=[t_sw1])
                P.op("vector", lambda v: v.tensor_copy(out=stg[:, 1:2], in_=self.sw2_bc[:]),
                     deps=[t_sw2])
                P.op("vector", lambda v: v.tensor_copy(out=stg[:, 2:3], in_=self.sw1inv_bc[:]))
                tv = P.op("vector", lambda v: v.tensor_copy(out=stg[:, 3:4], in_=self.sw2inv_bc[:]))
                self.final_store_toks.append(self.dma("sync", ap[:, :], stg[:], deps=[tv]))

            self.t_w0 = {en: P.token(en) for en in ENGINES}
            self.t_w0_dma = P.qtokens("sync") + P.qtokens("gpsimd")

    def phase_p1(self, b):
        nc, P = self.nc, self.P
        with ExitStack() as p1:
            pe = p1.enter_context
            xnT = [pe(nc.sbuf_tensor(f"p1xnT{i}", [128, N], BF16)) for i in range(DC)]
            tile_scope = ExitStack()
            pe2 = tile_scope.enter_context
            pstr = [pe2(nc.psum_tensor(f"p1pstr{i}", [128, 1024], BF16)) for i in range(4)]
            psrow = pe2(nc.psum_tensor("p1psrow", [1, 512], F32))
            xt = [pe2(nc.sbuf_tensor(f"p1x{i}", [128, D], F32)) for i in range(2)]
            xn = [pe2(nc.sbuf_tensor(f"p1xn{i}", [128, D], F32)) for i in range(2)]
            xnb = [pe2(nc.sbuf_tensor(f"p1xnb{i}", [128, D], BF16)) for i in range(2)]
            xq = [pe2(nc.sbuf_tensor(f"p1xq{i}", [128, D], BF16)) for i in range(2)]
            xi32 = pe2(nc.sbuf_tensor("p1xi32", [128, D], mybir.dt.int32))
            sq = xq
            st1 = [pe2(nc.sbuf_tensor(f"p1st1{i}", [128, 1], F32)) for i in range(2)]
            st2 = [pe2(nc.sbuf_tensor(f"p1st2{i}", [128, 1], F32)) for i in range(2)]
            stm = [pe2(nc.sbuf_tensor(f"p1stm{i}", [128, 1], F32)) for i in range(2)]
            stv = [pe2(nc.sbuf_tensor(f"p1stv{i}", [128, 1], F32)) for i in range(2)]
            strs = [pe2(nc.sbuf_tensor(f"p1strs{i}", [128, 1], F32)) for i in range(2)]
            stb = [pe2(nc.sbuf_tensor(f"p1stb{i}", [128, 1], F32)) for i in range(2)]
            qrs = [pe2(nc.sbuf_tensor(f"p1qrs{i}", [128, 1], F32)) for i in range(2)]
            qam = [pe2(nc.sbuf_tensor(f"p1qam{i}", [128, 1], F32)) for i in range(2)]
            qc = [pe2(nc.sbuf_tensor(f"p1qc{i}", [128, 1], F32)) for i in range(2)]
            qg = [pe2(nc.sbuf_tensor(f"p1qg{i}", [128, 1], F32)) for i in range(2)]
            sxv = pe2(nc.sbuf_tensor("p1sxv", [128, NT], F32))
            xqstg1 = pe2(nc.sbuf_tensor("p1xqstg", [128, 512], BF16))
            xqstg = [xqstg1, xqstg1]

            tok_done = []

            def gen_p1_tile(t):
                par = t % 2
                dep_buf = [tok_done[t - 2]] if t >= 2 else []
                tl = self.dma("sync", xt[par][:], self.x_in[b, ts128(t), :], deps=dep_buf)
                t_sum = P.op("vector", lambda v, par=par: v.reduce_sum(
                    out=st1[par][:], in_=xt[par][:], axis=AX), deps=[tl])
                yield
                t_sq = P.op("scalar", lambda sc, par=par: sc.activation(
                    out=sq[par][:], in_=xt[par][:], func=AF.Square, accum_out=st2[par][:]),
                    deps=[tl, P.token("vector")])
                yield
                t_m = P.op("vector", lambda v, par=par: v.tensor_scalar(
                    out=stm[par][:], in0=st1[par][:], scalar1=1.0 / D, scalar2=None,
                    op0=ALU.mult), deps=[t_sum], self_dep=True)
                yield
                t_m2 = P.op("vector", lambda v, par=par: v.scalar_tensor_tensor(
                    out=stv[par][:], in0=stm[par][:], scalar=-1.0, in1=stm[par][:],
                    op0=ALU.mult, op1=ALU.mult), deps=[t_m], self_dep=True)
                yield
                t_v = P.op("vector", lambda v, par=par: v.scalar_tensor_tensor(
                    out=stv[par][:], in0=st2[par][:], scalar=1.0 / D, in1=stv[par][:],
                    op0=ALU.mult, op1=ALU.add), deps=[t_sq, t_m2], self_dep=True)
                yield
                t_ve = P.op("vector", lambda v, par=par: v.tensor_scalar(
                    out=stv[par][:], in0=stv[par][:], scalar1=EPS_LN, scalar2=None,
                    op0=ALU.add), deps=[t_v], self_dep=True)
                yield
                t_ln = P.op("scalar", lambda sc, par=par: sc.activation(
                    out=strs[par][:], in_=stv[par][:], func=AF.Ln), deps=[t_ve])
                yield
                t_rs = P.op("scalar", lambda sc, par=par: sc.activation(
                    out=strs[par][:], in_=strs[par][:], func=AF.Exp, scale=-0.5),
                    deps=[t_ln], self_dep=True)
                yield
                t_b = P.op("vector", lambda v, par=par: v.scalar_tensor_tensor(
                    out=stb[par][:], in0=stm[par][:], scalar=-1.0, in1=strs[par][:],
                    op0=ALU.mult, op1=ALU.mult), deps=[t_rs])
                yield
                t_xn = P.op("scalar", lambda sc, par=par: sc.activation(
                    out=xn[par][:], in_=xt[par][:], func=AF.Identity, bias=stb[par][:],
                    scale=strs[par][:]), deps=[t_b])
                yield
                t_xnb = P.op("scalar", lambda sc, par=par: sc.copy(
                    out=xnb[par][:], in_=xn[par][:]), deps=[t_xn], self_dep=True)
                yield
                t_r1 = P.op("vector", lambda v, par=par: v.reciprocal(
                    out=qrs[par][:], in_=stv[par][:]), deps=[t_ve])
                yield
                t_u = P.op("vector", lambda v, par=par: v.tensor_scalar(
                    out=qrs[par][:], in0=qrs[par][:], scalar1=-EPS_LN, scalar2=1.0 + EPS_RMS,
                    op0=ALU.mult, op1=ALU.add), deps=[t_r1], self_dep=True)
                yield
                t_l2 = P.op("scalar", lambda sc, par=par: sc.activation(
                    out=qrs[par][:], in_=qrs[par][:], func=AF.Ln), deps=[t_u])
                yield
                t_rs2 = P.op("scalar", lambda sc, par=par: sc.activation(
                    out=qrs[par][:], in_=qrs[par][:], func=AF.Exp, scale=-0.5),
                    deps=[t_l2], self_dep=True)
                yield
                t_am = P.op("vector", lambda v, par=par: v.tensor_reduce(
                    out=qam[par][:], in_=xn[par][:], axis=AX, op=ALU.max,
                    apply_absolute_value=True), deps=[t_xn])
                yield
                t_c = P.op("vector", lambda v, par=par: v.scalar_tensor_tensor(
                    out=qc[par][:], in0=qam[par][:], scalar=1.0, in1=qrs[par][:],
                    op0=ALU.mult, op1=ALU.mult), deps=[t_am, t_rs2])
                yield
                t_c2 = P.op("vector", lambda v, par=par: v.tensor_scalar(
                    out=qc[par][:], in0=qc[par][:], scalar1=EPS_Q, scalar2=None,
                    op0=ALU.max), deps=[t_c], self_dep=True)
                yield
                t_rc = P.op("vector", lambda v, par=par: v.reciprocal(
                    out=qg[par][:], in_=qc[par][:]), deps=[t_c2], self_dep=True)
                yield
                t_sx = P.op("vector", lambda v, t=t, par=par: v.tensor_scalar(
                    out=sxv[:, t:t + 1], in0=qg[par][:], scalar1=127.0, scalar2=None,
                    op0=ALU.mult), deps=[t_rc], self_dep=True)
                yield
                t_c1 = P.op("vector", lambda v, t=t, b=b, par=par: v.scalar_tensor_tensor(
                    out=self.c1v[:, NT * b + t:NT * b + t + 1], in0=qc[par][:],
                    scalar=1.0 / 127.0, in1=self.sw1inv_bc[:],
                    op0=ALU.mult, op1=ALU.mult),
                    deps=[t_c2, self.t_w0["vector"]], self_dep=True)
                yield
                t_g = P.op("vector", lambda v, par=par: v.scalar_tensor_tensor(
                    out=qg[par][:], in0=qg[par][:], scalar=127.0, in1=qrs[par][:],
                    op0=ALU.mult, op1=ALU.mult), deps=[t_rc, t_sx], self_dep=True)
                yield
                t_q1 = P.op("vector", lambda v, par=par: v.tensor_scalar(
                    out=xi32[:], in0=xn[par][:], scalar1=qg[par][:], scalar2=None,
                    op0=ALU.mult), deps=[t_g, t_xn, t_sq])
                yield
                t_q2 = P.op("vector", lambda v, par=par: v.tensor_copy(
                    out=xq[par][:], in_=xi32[:]), deps=[t_q1], self_dep=True)
                yield
                tr_toks = []
                for q in range(2):
                    tq_ = None
                    for k in range(4):
                        i = 4 * q + k
                        tq_ = P.op("tensor", lambda t_, par=par, i=i, k=k: t_.transpose(
                            pstr[k][:, 0:128], xnb[par][:, ts128(i)], self.ident[:]),
                            deps=[t_xnb, P.token("scalar")], drain=(k == 3))
                    for k in range(4):
                        i = 4 * q + k
                        tv = P.op("scalar", lambda sc, i=i, t=t, k=k: sc.copy(
                            out=xnT[i][:, ts128(t)], in_=pstr[k][:, 0:128]), deps=[tq_])
                        tr_toks.append(tv)
                    yield
                for q in range(2):
                    sp = q % 2
                    tq_ = None
                    for k in range(4):
                        i = 4 * q + k
                        tq_ = P.op("tensor", lambda t_, par=par, i=i, k=k: t_.transpose(
                            pstr[k][:, 0:128], xq[par][:, ts128(i)], self.ident[:]),
                            deps=[t_q2, P.token("scalar")] + P.qtokens("gpsimd"),
                            drain=(k == 3))
                    evs = []
                    for k in range(4):
                        tv = P.op("scalar", lambda sc, k=k, sp=sp: sc.copy(
                            out=xqstg[sp][:, 128 * k:128 * (k + 1)],
                            in_=pstr[k][:, 0:128]), deps=[tq_])
                        evs.append(tv)
                        tr_toks.append(tv)
                    for k in range(4):
                        i = 4 * q + k
                        td = self.dma("gpsimd", self.xqT_s[b, ts128(i), ts128(t)],
                                      xqstg[sp][:, 128 * k:128 * (k + 1)],
                                      deps=[evs[k]])
                    yield
                tok_done.append(tr_toks[-1])

            prev_g = None
            for t in range(NT):
                g = gen_p1_tile(t)
                if prev_g is None:
                    next(g)
                    next(g)
                    prev_g = g
                    continue
                while True:
                    try:
                        next(prev_g)
                    except StopIteration:
                        break
                    try:
                        next(g)
                    except StopIteration:
                        pass
                prev_g = g
            for _ in prev_g:
                pass

            t_p1_tiles = [P.token("scalar"), P.token("vector")]

            t_mulsw = P.op("vector", lambda v: v.tensor_scalar(
                out=sxv[:], in0=sxv[:], scalar1=self.sw1_bc[:], scalar2=None, op0=ALU.mult),
                deps=t_p1_tiles + [self.t_w0["vector"]], self_dep=True)
            t_sxT = []
            for t in range(NT):
                tt = P.op("tensor", lambda t_, t=t: t_.transpose(
                    psrow[:, 0:128], sxv[:, t:t + 1], self.identf[:]),
                    deps=[t_mulsw, P.token("scalar")], drain=True)
                tv = P.op("scalar", lambda sc, t=t, b=b: sc.copy(
                    out=self.sxw_row[0:1, N * b + 128 * t:N * b + 128 * (t + 1)],
                    in_=psrow[0:1, 0:128]), deps=[tt])
                t_sxT.append(tv)
            self.t_sxw = t_sxT[-1]

            for i_ in range(DC):
                self.dump(f"xnT{i_}", xnT[i_][:], (128, N), BF16, [tok_done[-1]])
            self.dump("c1v", self.c1v[:, NT * b:NT * (b + 1)], (128, NT), F32, t_p1_tiles)
            self.dump("sxwrow", self.sxw_row[0:1, N * b:N * (b + 1)], (1, N), BF16, [self.t_sxw])

            tile_scope.close()
            P.fence()
            psq = [pe(nc.psum_tensor(f"p1psq{i}", [128, 512], F32)) for i in range(2)]
            psv = [pe(nc.psum_tensor(f"p1psv{i}", [128, 512], F32)) for i in range(2)]
            wsec = [pe(nc.sbuf_tensor(f"p1w{i}", [128, D], BF16)) for i in range(DC)]
            wsec2 = wsec
            wsec3 = wsec
            ev_bank = {0: None, 1: None}

            def project_T(wT_dram, bias_chunks, dstT, deps_w, wsec):
                P.fence()
                lt = []
                for i in range(DC):
                    lt.append(self.dma("sync", wsec[i][:], wT_dram[ts128(i), :], deps=deps_w))
                toks = []
                for j in range(DC):
                    for blk in range(4):
                        pp = (j * 4 + blk) % 2
                        mm = None
                        for i in range(DC):
                            mm = P.op("tensor", lambda t_, i=i, j=j, blk=blk, pp=pp: t_.matmul(
                                psq[pp][:, :], wsec[i][:, ts128(j)],
                                xnT[i][:, 512 * blk:512 * (blk + 1)],
                                start=(i == 0), stop=(i == DC - 1)),
                                deps=[lt[i], tok_done[-1], ev_bank[pp]],
                                drain=(i == DC - 1))
                        tv = P.op("scalar", lambda sc, j=j, blk=blk, pp=pp: sc.activation(
                            out=dstT[j][:, 512 * blk:512 * (blk + 1)], in_=psq[pp][:, :],
                            func=AF.Identity, bias=bias_chunks[:, j:j + 1]), deps=[mm])
                        ev_bank[pp] = tv
                        toks.append(tv)
                return toks

            tq = project_T(self.wqT_s, self.bqs, self.QT, [self.t_w0["sync"]], wsec)
            tk = project_T(self.wkT_s, self.bks, self.KT, [P.token("tensor")], wsec2)

            P.fence()
            lt = []
            for i in range(DC):
                lt.append(self.dma("sync", wsec3[i][:], self.wvT_s[ts128(i), :],
                                   deps=[P.token("tensor")]))
            ev_half = {0: tk[-1], 1: tk[-1]}
            dbg_v = b == 0 and "p1_psv0" in self.debug_names
            if dbg_v:
                vdbg = pe(nc.sbuf_tensor("p1vdbg", [128, 512], F32))
            tv_toks = []
            for t in range(NT):
                for half in range(2):
                    sl = slice(512 * half, 512 * (half + 1))
                    mm = None
                    for i in range(DC):
                        mm = P.op("tensor", lambda t_, i=i, t=t, sl=sl, half=half: t_.matmul(
                            psv[half][:, :], xnT[i][:, ts128(t)], wsec3[i][:, sl],
                            start=(i == 0), stop=False),
                            deps=[lt[i], ev_half[half]])
                    mm = P.op("tensor", lambda t_, sl=sl, half=half: t_.matmul(
                        psv[half][:, :], self.onesrow[:, 0:128], self.bvrow[:, sl],
                        start=False, stop=True), deps=[mm, self.t_ones] + self.t_const_g,
                        drain=True)
                    if dbg_v and t == 0 and half == 0:
                        tcp = P.op("scalar", lambda sc: sc.copy(
                            out=vdbg[:], in_=psv[0][:, :]), deps=[mm])
                        self.dump("p1_psv0", vdbg[:], (128, 512), F32, [tcp])
                    tvv = P.op("scalar", lambda sc, t=t, sl=sl, half=half: sc.copy(
                        out=self.V[t][:, sl], in_=psv[half][:, :]), deps=[mm])
                    ev_half[half] = tvv
                    tv_toks.append(tvv)
            if dbg_v:
                for i_ in range(DC):
                    self.dump(f"p1_wv{i_}", wsec3[i_][:], (128, D), BF16, [tv_toks[-1]])

            self.t_v_done = tv_toks[-1]
            self.dump("QT0", self.QT[0][:], (128, N), BF16, [tq[-1]])
            self.dump("KT0", self.KT[0][:], (128, N), BF16, [tk[-1]])
            self.dump("V0", self.V[0][:], (128, D), BF16, [tv_toks[1]])
            self.dump("V15", self.V[15][:], (128, D), BF16, [tv_toks[-1]])

            self.t_p1 = {en: P.token(en) for en in ENGINES}
            self.t_p1_spill = P.qtokens("gpsimd")

    def phase_p2(self, b):
        nc, P = self.nc, self.P
        with ExitStack() as p2:
            pe = p2.enter_context
            qrow = [pe(nc.sbuf_tensor(f"p2qr{i}", [1, N], BF16)) for i in range(2)]
            s1 = [pe(nc.sbuf_tensor(f"p2s1_{i}", [128, N], F32)) for i in range(2)]
            pTS = pe(nc.sbuf_tensor("p2pTS", [128, 16 * 512], BF16))
            ebuf = [pe(nc.sbuf_tensor(f"p2eb{i}", [128, 512], F32)) for i in range(2)]
            m8c = [pe(nc.sbuf_tensor(f"p2m8{i}", [128, 32 * NT], F32)) for i in range(2)]
            e8b = pe(nc.sbuf_tensor("p2e8", [128, 32], F32))
            z8t = pe(nc.sbuf_tensor("p2z8t", [128, 32], F32))
            zv = [pe(nc.sbuf_tensor(f"p2zv{i}", [128, NT], F32)) for i in range(2)]
            rzv = [pe(nc.sbuf_tensor(f"p2rz{i}", [128, NT], F32)) for i in range(2)]
            ntq = pe(nc.sbuf_tensor("p2ntq", [128, NT], F32))
            tt1 = pe(nc.sbuf_tensor("p2tt1", [128, 1], F32))
            tt2 = pe(nc.sbuf_tensor("p2tt2", [128, 1], F32))
            tbf = pe(nc.sbuf_tensor("p2tbf", [128, 1], BF16))
            st16 = pe(nc.sbuf_tensor("p2st16", [16, 128], BF16))
            ps_sc = pe(nc.psum_tensor("p2sc", [128, 2048], F32))
            ps_sT = [pe(nc.psum_tensor(f"p2sT{i}", [128, 512], F32)) for i in range(2)]
            ps_ao = pe(nc.psum_tensor("p2ao", [128, 512], F32))
            ps_tr = pe(nc.psum_tensor("p2tr", [128, 1024], BF16))
            aor = [pe(nc.sbuf_tensor(f"p2aor{i}", [128, 512], BF16)) for i in range(2)]

            free_aug = [None, None]
            free_s1 = [None, None]
            free_m8 = [None, None]
            free_zv = [None, None]
            t_pT_free = None
            ebuf_free = {0: None, 1: None}
            t_aor0 = [P.op("vector", lambda v, i=i: v.memset(aor[i][64:128, :], 0.0))
                      for i in range(2)]
            t_raw_ev = [t_aor0[0], t_aor0[1]]
            ev_tr = None
            t_at_ev = None
            t_mr1 = None
            t_st_prev = None
            t_qrow_prev = None
            t_tr_prev = None
            last_sT_reader = {0: None, 1: None}

            t_qrow_by_par = [None, None]
            t_rz_by_par = [None, None]
            t_mr1 = t_mr1
            t_last_atmm = None

            def emit_A(h):
                nonlocal t_mr1, t_st_prev, t_qrow_prev, t_tr_prev
                par = h % 2
                ch = h // 2
                rows = slice(64 * (h % 2), 64 * (h % 2) + 64)
                t_kq = [self.t_p1["scalar"], free_aug[par]]

                t_e8 = None
                last_tup = None
                thresh_t = {"e8": None}

                def tilegen(t):
                    nonlocal t_mr1, t_tr_prev, last_tup, t_e8
                    k_len = 128 * (t + 1)
                    bi = t % 2
                    nblk = (k_len + 511) // 512
                    mm = None
                    for blk in range(nblk):
                        nn = min(512, k_len - 512 * blk)
                        mm = P.op("tensor", lambda t_, t=t, blk=blk, nn=nn, ch=ch, rows=rows: t_.matmul(
                            ps_sc[:, 512 * blk:512 * blk + nn],
                            self.QT[ch][rows, ts128(t)],
                            self.KT[ch][rows, 512 * blk:512 * blk + nn],
                            start=True, stop=(blk != nblk - 1)),
                            deps=[t_kq[0], t_kq[1], t_mr1, t_st_prev])
                    j = t % 4
                    nlast = k_len - 512 * (nblk - 1)
                    mm = P.op("tensor", lambda t_, t=t, j=j, nblk=nblk, nlast=nlast: t_.matmul(
                        ps_sc[:, 512 * (nblk - 1):512 * (nblk - 1) + nlast],
                        self.triT[:, :],
                        self.ew[:, 384 - 128 * j:384 - 128 * j + nlast],
                        start=False, stop=True),
                        deps=[mm], drain=True)
                    c0 = 32 * t
                    r1 = P.op("vector", lambda v, par=par, k_len=k_len, c0=c0: v.max(
                        out=m8c[par][:, c0:c0 + 8], in_=ps_sc[:, 0:k_len]),
                        deps=[mm, free_m8[par]])
                    yield
                    t_mr1 = P.op("vector", lambda v, bi=bi, par=par, k_len=k_len, c0=c0: v.match_replace(
                        out=s1[bi][:, 0:k_len], in_to_replace=m8c[par][:, c0:c0 + 8],
                        in_values=ps_sc[:, 0:k_len], imm_value=NEG),
                        deps=[r1, free_s1[bi]], self_dep=True)
                    yield
                    last = t_mr1
                    for rnd in range(1, 4):
                        rr = P.op("vector", lambda v, bi=bi, par=par, rnd=rnd, k_len=k_len, c0=c0: v.max(
                            out=m8c[par][:, c0 + 8 * rnd:c0 + 8 * rnd + 8],
                            in_=s1[bi][:, 0:k_len]),
                            deps=[last], self_dep=True)
                        yield
                        if rnd < 3:
                            last = P.op("vector",
                                        lambda v, bi=bi, par=par, rnd=rnd, k_len=k_len, c0=c0: v.match_replace(
                                            out=s1[bi][:, 0:k_len],
                                            in_to_replace=m8c[par][:, c0 + 8 * rnd:c0 + 8 * rnd + 8],
                                            in_values=s1[bi][:, 0:k_len], imm_value=NEG),
                                        deps=[rr], self_dep=True)
                            yield
                        else:
                            last = rr
                    free_s1[bi] = last
                    sel_last = last
                    tmin = P.op("vector", lambda v, par=par, c0=c0: v.tensor_reduce(
                        out=tt1[:], in_=m8c[par][:, c0 + 24:c0 + 32], axis=AX, op=ALU.min),
                        deps=[sel_last, thresh_t["e8"]])
                    yield
                    ttop = P.op("vector", lambda v, par=par, c0=c0: v.tensor_scalar(
                        out=tt2[:], in0=m8c[par][:, c0:c0 + 1], scalar1=-GAP, scalar2=None,
                        op0=ALU.add), deps=[tmin], self_dep=True)
                    yield
                    tmax = P.op("vector", lambda v: v.tensor_max(
                        out=tt1[:], in0=tt1[:], in1=tt2[:]), deps=[ttop], self_dep=True)
                    yield
                    tcb = P.op("vector", lambda v: v.tensor_copy(out=tbf[:], in_=tt1[:]),
                               deps=[tmax], self_dep=True)
                    yield
                    tup = P.op("vector", lambda v, t=t: v.tensor_scalar(
                        out=ntq[:, t:t + 1], in0=tbf[:], scalar1=-1.0, scalar2=None,
                        op0=ALU.mult), deps=[tcb, t_tr_prev], self_dep=True)
                    last_tup = tup
                    yield
                    t_e8s = P.op("scalar", lambda sc, par=par, t=t, c0=c0: sc.activation(
                        out=e8b[:], in_=m8c[par][:, c0:c0 + 32], func=AF.Exp,
                        bias=ntq[:, t:t + 1]), deps=[sel_last, tup, thresh_t["e8"]])
                    yield
                    t_e8 = P.op("vector", lambda v, par=par, t=t: v.scalar_tensor_tensor(
                        out=z8t[:], in0=e8b[:], scalar=1.0, in1=e8b[:],
                        op0=ALU.is_ge, op1=ALU.mult, accum_out=zv[par][:, t:t + 1]),
                        deps=[t_e8s, free_zv[par]])
                    thresh_t["e8"] = t_e8

                PRETH = 7
                prev_g = None
                for t in range(NT):
                    g = tilegen(t)
                    if prev_g is None:
                        next(g)
                        next(g)
                        prev_g = g
                        continue
                    gsteps = 0
                    prev_alive = True
                    while prev_alive:
                        try:
                            next(prev_g)
                        except StopIteration:
                            prev_alive = False
                            break
                        if gsteps < PRETH:
                            try:
                                next(g)
                                gsteps += 1
                            except StopIteration:
                                pass
                    while gsteps < 2:
                        next(g)
                        gsteps += 1
                    prev_g = g
                for _ in prev_g:
                    pass
                free_m8[par] = t_e8
                t_rz = P.op("vector", lambda v, par=par: v.reciprocal(
                    out=rzv[par][:], in_=zv[par][:]), deps=[t_e8, t_at_ev], self_dep=True)
                t_rz_by_par[par] = t_rz
                free_zv[par] = t_rz
                t_tr = P.op("tensor", lambda t_: t_.transpose(
                    ps_sc[0:16, 0:128], ntq[:, 0:NT], self.identf[:]),
                    deps=[t_mr1, last_tup, P.token("scalar")], drain=True)
                t_st = P.op("scalar", lambda sc: sc.copy(out=st16[:], in_=ps_sc[0:16, 0:128]),
                            deps=[t_tr, t_qrow_prev])
                t_qrow = self.dma("sync", qrow[par][0:1, :], st16[:, :],
                                  deps=[t_st, free_aug[par]])
                t_qrow_by_par[par] = t_qrow
                t_st_prev = t_st
                t_qrow_prev = t_qrow
                t_tr_prev = t_tr
                if h == 0 and b == 0 and "h0_m8" in self.debug_names:
                    d1 = self.dump("h0_m8", m8c[0][:, 32 * 15:32 * 16], (128, 32), F32,
                                   [t_e8])
                    d2 = self.dump("h0_zv", zv[0][:], (128, NT), F32, [t_e8])
                    d3 = self.dump("h0_ntq", ntq[:], (128, NT), F32, [t_e8])
                    d5 = self.dump("h0_rzv", rzv[0][:], (128, NT), F32, [t_rz])
                    free_m8[par] = d1
                    free_zv[par] = d2
                    t_tr_prev = d3
                    self._h0_rz_dump = d5

            def emit_B(h):
                nonlocal t_pT_free, ev_tr, t_at_ev, t_last_atmm
                par = h % 2
                ch = h // 2
                rows = slice(64 * (h % 2), 64 * (h % 2) + 64)
                t_qrow = t_qrow_by_par[par]
                t_rz = t_rz_by_par[par]
                t_last_atmm = None
                pending_tr = []
                for S in range(4):
                    q0 = 512 * S
                    nC = 4 * S + 4
                    t_pT = None
                    for c0 in range(0, nC, 2):
                        if pending_tr:
                            pending_tr.pop(0)()
                        mmp = None
                        for cp in range(2):
                            c = c0 + cp
                            mmp = P.op("tensor", lambda t_, c=c, q0=q0, ch=ch, rows=rows, cp=cp: t_.matmul(
                                ps_sT[cp][:, :], self.KT[ch][rows, ts128(c)],
                                self.QT[ch][rows, q0:q0 + 512], start=True, stop=False),
                                deps=[t_qrow, last_sT_reader[cp], self.t_v_done])
                            mmp = P.op("tensor", lambda t_, q0=q0, par=par, cp=cp: t_.matmul(
                                ps_sT[cp][:, :], self.onesrow[:, 0:128],
                                qrow[par][0:1, q0:q0 + 512], start=False, stop=True),
                                deps=[mmp, self.t_ones],
                                drain=(cp == 1))
                        for cp in range(2):
                            c = c0 + cp
                            te = P.op("scalar", lambda sc, cp=cp: sc.activation(
                                out=ebuf[cp][:], in_=ps_sT[cp][:, :], func=AF.Exp),
                                deps=[mmp, ebuf_free[cp]])
                            last_sT_reader[cp] = te
                            last_c = c == nC - 1
                            tp = P.op("vector", lambda v, c=c, cp=cp: v.scalar_tensor_tensor(
                                out=pTS[:, 512 * c:512 * (c + 1)], in0=ebuf[cp][:],
                                scalar=1.0, in1=ebuf[cp][:],
                                op0=ALU.is_ge, op1=ALU.mult),
                                deps=[te, t_pT_free],
                                drain=last_c and c < 4 * S)
                            ebuf_free[cp] = tp
                            if c >= 4 * S:
                                jj = c - 4 * S
                                if jj > 0:
                                    tp = P.op("vector", lambda v, c=c, jj=jj: v.memset(
                                        pTS[:, 512 * c:512 * c + 128 * jj], 0.0),
                                        deps=[tp], self_dep=True)
                                tp = P.op("vector", lambda v, c=c, jj=jj: v.tensor_tensor(
                                    out=pTS[:, 512 * c + 128 * jj:512 * c + 128 * (jj + 1)],
                                    in0=pTS[:, 512 * c + 128 * jj:512 * c + 128 * (jj + 1)],
                                    in1=self.tri01[:, :], op=ALU.mult),
                                    deps=[tp], self_dep=True, drain=last_c)
                            t_pT = tp
                    sp = S % 2
                    mm = None
                    for c in range(4 * S + 4):
                        mm = P.op("tensor", lambda t_, c=c, S=S, h=h: t_.matmul(
                            ps_ao[0:64, 0:512],
                            self.V[c][:, 64 * h:64 * h + 64],
                            pTS[:, 512 * c:512 * (c + 1)],
                            start=(c == 0), stop=(c == 4 * S + 3)),
                            deps=[t_pT, t_raw_ev[0], t_raw_ev[1],
                                  self.t_p1["scalar"]],
                            drain=(c == 4 * S + 3))
                        t_last_atmm = mm
                    t_raw = P.op("scalar", lambda sc, sp=sp: sc.copy(
                        out=aor[sp][0:64, :], in_=ps_ao[0:64, 0:512]),
                        deps=[mm, ev_tr])
                    t_raw_ev[sp] = t_raw

                    def mk_tr(j, sp, S, t_raw):
                        def emit():
                            nonlocal ev_tr, t_at_ev
                            T = 4 * S + j
                            ttr = P.op("tensor", lambda t_, j=j, sp=sp: t_.transpose(
                                ps_tr[:, 0:128], aor[sp][:, 128 * j:128 * (j + 1)],
                                self.ident[:]), deps=[t_raw, ev_tr], drain=True)
                            t_at_ev = P.op("scalar", lambda sc, T=T, h=h, par=par: sc.activation(
                                out=self.AO[T][:, 64 * h:64 * h + 64], in_=ps_tr[:, 0:64],
                                func=AF.Identity, scale=rzv[par][:, T:T + 1]),
                                deps=[ttr, t_rz])
                            ev_tr = t_at_ev
                        return emit
                    for j in range(4):
                        pending_tr.append(mk_tr(j, sp, S, t_raw))
                    t_pT_free = t_last_atmm
                for fn in pending_tr:
                    fn()
                free_aug[par] = t_last_atmm
                if h == 0 and b == 0 and "h0_m8" in self.debug_names:
                    d4 = self.dump("h0_pTS", pTS[:], (128, 16 * 512), BF16, [t_last_atmm])
                    self.dump("h0_ao12", self.AO[12][:, 0:64], (128, 64), BF16, [t_at_ev])
                    self.dump("h0_ao15", self.AO[15][:, 0:64], (128, 64), BF16, [t_at_ev])
                    for t_ in range(NT):
                        if f"h0_ao{t_}" in self.debug_names:
                            self.dump(f"h0_ao{t_}", self.AO[t_][:, 0:64], (128, 64),
                                      BF16, [t_at_ev])
                    for c_ in range(NT):
                        self.dump(f"h0_V{c_}", self.V[c_][:, 0:64], (128, 64), BF16,
                                  [self.t_p1["scalar"]])
                    t_pT_free = d4
                    free_aug[par] = self._h0_rz_dump

            emit_A(0)
            emit_A(1)
            for h in range(NH):
                emit_B(h)
                if h + 2 < NH:
                    emit_A(h + 2)
            self.t_p2 = {en: P.token(en) for en in ENGINES}
            self.t_p2_qsync = P.qtokens("sync")

    def phase_p2c(self, b):
        nc, P = self.nc, self.P
        with ExitStack() as p2c:
            pe = p2c.enter_context
            aoT = [pe(nc.sbuf_tensor(f"c_aoT{i}", [128, N], BF16)) for i in range(DC)]
            wsec = [pe(nc.sbuf_tensor(f"c_w{i}", [128, D], BF16)) for i in range(DC)]
            stile = [pe(nc.sbuf_tensor(f"c_st{i}", [128, D], BF16)) for i in range(2)]
            ps_mb4 = [pe(nc.psum_tensor(f"c_mb{i}", [128, 1024], BF16)) for i in range(4)]
            ps_o = [pe(nc.psum_tensor(f"c_o{i}", [128, 512], F32)) for i in range(2)]

            lt = []
            for i in range(DC):
                lt.append(self.dma("sync", wsec[i][:], self.woT_s[ts128(i), :],
                                   deps=self.t_w0_dma))
            t_tr_done = []
            tr_ev4 = [None, None, None, None]
            for t in range(NT):
                for q in range(DC // 4):
                    tq_ = None
                    for k in range(4):
                        i = 4 * q + k
                        tq_ = P.op("tensor", lambda t_, t=t, i=i, k=k: t_.transpose(
                            ps_mb4[k][:, 0:128], self.AO[t][:, ts128(i)],
                            self.ident[:]),
                            deps=[self.t_p2["scalar"], tr_ev4[k]], drain=(k == 3))
                    for k in range(4):
                        i = 4 * q + k
                        tv = P.op("scalar", lambda sc, t=t, i=i, k=k: sc.copy(
                            out=aoT[i][:, ts128(t)], in_=ps_mb4[k][:, 0:128]),
                            deps=[tq_])
                        tr_ev4[k] = tv
                        t_tr_done.append(tv)
            ev_bank = {0: None, 1: None}
            t_sp = [None, None]
            for t in range(NT):
                par = t % 2
                evs = []
                for blk in range(2):
                    pp = (2 * t + blk) % 2
                    mm = None
                    for i in range(DC):
                        mm = P.op("tensor", lambda t_, t=t, i=i, blk=blk, pp=pp: t_.matmul(
                            ps_o[pp][:, :], aoT[i][:, ts128(t)],
                            wsec[i][:, 512 * blk:512 * (blk + 1)],
                            start=(i == 0), stop=False),
                            deps=[lt[i], t_tr_done[-1], ev_bank[pp]])
                    mm = P.op("tensor", lambda t_, blk=blk, pp=pp: t_.matmul(
                        ps_o[pp][:, :], self.onesrow[:, 0:128],
                        self.borow[:, 512 * blk:512 * (blk + 1)],
                        start=False, stop=True),
                        deps=[mm, self.t_ones] + self.t_const_g, drain=True)
                    tv = P.op("scalar", lambda sc, t=t, blk=blk, pp=pp, par=par: sc.copy(
                        out=stile[par][:, 512 * blk:512 * (blk + 1)], in_=ps_o[pp][:, :]),
                        deps=[mm, t_sp[par]])
                    ev_bank[pp] = tv
                    evs.append(tv)
                t_sp[par] = self.dma("gpsimd", self.s_s[b, ts128(t), :], stile[par][:],
                                     deps=evs)
            self.t_p2c = {en: P.token(en) for en in ENGINES}
            self.t_p2c_dma = P.qtokens("gpsimd")
            if "s_scr" in self.debug_names and b == 0:
                ap = self.dbg_out("s_scr", (N, D), BF16)
                self.final_store_toks.append(
                    self.dma("sync", ap[:, :], self.s_s[0], deps=self.t_p2c_dma))
            for t_ in range(NT):
                if f"ao{t_}" in self.debug_names and b == 0:
                    self.dump(f"ao{t_}", self.AO[t_][:], (128, D), BF16,
                              [self.t_p2["scalar"]])

    def phase_p3(self, b):
        nc, P = self.nc, self.P
        with ExitStack() as p3o:
            po = p3o.enter_context
            msqv = po(nc.sbuf_tensor("p3msq", [128, NT], F32))
            amaxv = po(nc.sbuf_tensor("p3amax", [128, NT], F32))
            with ExitStack() as p3a:
                pa = p3a.enter_context
                W1 = [pa(nc.sbuf_tensor(f"a_W1{i}", [128, HID], BF16)) for i in range(DC)]
                b1row = pa(nc.sbuf_tensor("a_b1row", [1, HID], BF16))
                xqs = [pa(nc.sbuf_tensor(f"a_xq{i}", [128, 256], BF16)) for i in range(DC)]
                htile = pa(nc.sbuf_tensor("a_h", [128, HID], F32))
                sqt = pa(nc.sbuf_tensor("a_sq", [128, HID], F32))
                hq = pa(nc.sbuf_tensor("a_hq", [128, HID], BF16))
                hqT = [pa(nc.sbuf_tensor(f"a_hqT{i}", [128, HID], BF16)) for i in range(2)]
                g2 = pa(nc.sbuf_tensor("a_g2", [128, 1], F32))
                hi32 = pa(nc.sbuf_tensor("a_hi32", [128, HID], mybir.dt.int32))
                ps_f = [pa(nc.psum_tensor(f"a_ps{i}", [128, 512], F32)) for i in range(2)]
                ps_mb4 = [pa(nc.psum_tensor(f"a_mb{i}", [128, 1024], BF16))
                          for i in range(4)]

                t_b1 = self.dma("gpsimd", b1row[:], self.b1_in[None, :])
                ltw = []
                for i in range(DC):
                    ltw.append(self.dma("sync", W1[i][:], self.w1qT_s[ts128(i), :],
                                        deps=self.t_w0_dma))
                ev_bank = {0: None, 1: None}
                t_hqT_dma = [None, None]
                t_prev_q = [None, None]
                t_hqtr_prev = None
                t_q1_prev = None
                for t in range(NT):
                    par = t % 2
                    ltx = []
                    for i in range(DC):
                        ltx.append(self.dma("sync", xqs[i][:, 128 * par:128 * (par + 1)],
                                            self.xqT_s[b, ts128(i), ts128(t)],
                                            deps=[t_prev_q[par],
                                                  getattr(self, "_p3dbg_xq", None)]
                                            + self.t_p1_spill))
                    t_gel = []
                    for hb in range(DC):
                        pp = hb % 2
                        mm = None
                        for i in range(DC):
                            mm = P.op("tensor", lambda t_, i=i, hb=hb, par=par, pp=pp: t_.matmul(
                                ps_f[pp][:, :], xqs[i][:, 128 * par:128 * (par + 1)],
                                W1[i][:, 512 * hb:512 * (hb + 1)],
                                start=(i == 0), stop=False),
                                deps=[ltx[i], ltw[i], ev_bank[pp]])
                        mm = P.op("tensor", lambda t_, hb=hb, t=t, pp=pp, b=b: t_.matmul(
                            ps_f[pp][:, :], self.sxw_row[0:1, N * b + 128 * t:N * b + 128 * (t + 1)],
                            b1row[0:1, 512 * hb:512 * (hb + 1)],
                            start=False, stop=True),
                            deps=[mm, t_b1, self.t_sxw], drain=True)
                        if (t == 0 and hb == 0 and b == 0
                                and "p3_dbg" in self.debug_names):
                            if "p3_ps0" in self.debug_names:
                                tcp = P.op("scalar", lambda sc: sc.copy(
                                    out=sqt[:, 0:512], in_=ps_f[pp][:, :]), deps=[mm])
                                self._p3dbg_ps = self.dump(
                                    "p3_ps0", sqt[:, 0:512], (128, 512), F32, [tcp])
                            self._p3dbg_xq = self.dump("p3_xq00", xqs[0][:], (128, 256),
                                                       BF16, [mm])
                            self.dump("p3_w1s0", W1[0][:], (128, HID), BF16, [mm])
                            self.dump("p3_sxw0", self.sxw_row[0:1, 0:N], (1, N), BF16,
                                      [mm])
                            self.dump("p3_b1r", b1row[:], (1, HID), BF16, [mm])
                            if "xq_scr" in self.debug_names:
                                ap = self.dbg_out("xq_scr", (D, N), BF16)
                                self.final_store_toks.append(self.dma(
                                    "sync", ap[:, :], self.xqT_s[0],
                                    deps=self.t_p1_spill))
                        tg = P.op("scalar", lambda sc, hb=hb, t=t, pp=pp, b=b: sc.activation(
                            out=htile[:, 512 * hb:512 * (hb + 1)], in_=ps_f[pp][:, :],
                            func=AF.Gelu, scale=self.c1v[:, NT * b + t:NT * b + t + 1]),
                            deps=[mm, t_q1_prev])
                        ev_bank[pp] = tg
                        t_gel.append(tg)
                    if t == 0 and b == 0 and "p3_h0" in self.debug_names:
                        t_hdump = self.dump("p3_h0", htile[:], (128, HID), F32, t_gel)
                        ev_bank = {0: t_hdump, 1: t_hdump}
                    t_sq = P.op("scalar", lambda sc, t=t: sc.activation(
                        out=sqt[:], in_=htile[:], func=AF.Square,
                        accum_out=msqv[:, t:t + 1]),
                        deps=t_gel + [getattr(self, "_p3dbg_ps", None)])
                    t_am = P.op("vector", lambda v, t=t: v.tensor_reduce(
                        out=amaxv[:, t:t + 1], in_=htile[:], axis=AX, op=ALU.max,
                        apply_absolute_value=True), deps=t_gel)
                    t_g2 = P.op("vector", lambda v, t=t: v.tensor_scalar(
                        out=g2[:], in0=amaxv[:, t:t + 1], scalar1=1e-30, scalar2=None,
                        op0=ALU.max), deps=[t_am], self_dep=True)
                    t_g2 = P.op("vector", lambda v: v.reciprocal(out=g2[:], in_=g2[:]),
                                deps=[t_g2], self_dep=True)
                    t_g2 = P.op("vector", lambda v: v.tensor_scalar(
                        out=g2[:], in0=g2[:], scalar1=127.0, scalar2=None, op0=ALU.mult),
                        deps=[t_g2], self_dep=True)
                    t_q1 = P.op("vector", lambda v: v.tensor_scalar(
                        out=hi32[:], in0=htile[:], scalar1=g2[:], scalar2=None,
                        op0=ALU.mult), deps=[t_g2])
                    t_q2 = P.op("vector", lambda v: v.tensor_copy(
                        out=hq[:], in_=hi32[:]),
                        deps=[t_q1, t_hqtr_prev], self_dep=True)
                    t_q1_prev = t_q1
                    trev = []
                    for q in range(HC // 4):
                        tq_ = None
                        for k in range(4):
                            i = 4 * q + k
                            ttr = P.op("tensor", lambda t_, i=i, k=k: t_.transpose(
                                ps_mb4[k][:, 0:128], hq[:, ts128(i)],
                                self.ident[:]),
                                deps=[t_q2, P.token("scalar"), t_hqT_dma[par]],
                                drain=(k == 3))
                            tq_ = ttr
                        for k in range(4):
                            i = 4 * q + k
                            tv = P.op("scalar", lambda sc, i=i, k=k, par=par: sc.copy(
                                out=hqT[par][:, ts128(i)], in_=ps_mb4[k][:, 0:128]),
                                deps=[tq_])
                            trev.append(tv)
                    t_hqT_dma[par] = self.dma("gpsimd", self.hq_t_s[t][:, :],
                                              hqT[par][:], deps=trev)
                    t_prev_q[par] = t_gel[-1]
                    t_hqtr_prev = trev[-1]
                    if t == 0 and b == 0 and "p3_hq0" in self.debug_names:
                        t_hqtr_prev = self.dump("p3_hq0", hq[:], (128, HID), BF16,
                                                [t_q2] + trev)
                self.t_p3a_g = P.qtokens("gpsimd")
            P.fence()
            with ExitStack() as p3b:
                pb = p3b.enter_context
                W2 = [pb(nc.sbuf_tensor(f"b_W2{i}", [128, D], BF16)) for i in range(HC)]
                b2row = pb(nc.sbuf_tensor("b_b2row", [1, D], BF16))
                ic2row = pb(nc.sbuf_tensor("b_ic2row", [1, N], BF16))
                hqt = [pb(nc.sbuf_tensor(f"b_hqt{i}", [128, HID], BF16)) for i in range(2)]
                xt = [pb(nc.sbuf_tensor(f"b_x{i}", [128, D], F32)) for i in range(2)]
                stl = [pb(nc.sbuf_tensor(f"b_s{i}", [128, D], BF16)) for i in range(2)]
                gt = pb(nc.sbuf_tensor("b_g", [128, D], F32))
                ot = [pb(nc.sbuf_tensor(f"b_o{i}", [128, D], F32)) for i in range(2)]
                rs2v = pb(nc.sbuf_tensor("b_rs2", [128, NT], F32))
                c2v = pb(nc.sbuf_tensor("b_c2", [128, NT], F32))
                ic2v = pb(nc.sbuf_tensor("b_ic2", [128, NT], F32))
                st16b = pb(nc.sbuf_tensor("b_st16", [16, 128], BF16))
                ps_f = [pb(nc.psum_tensor(f"b_ps{i}", [128, 512], F32)) for i in range(2)]
                ps_r = pb(nc.psum_tensor("b_psr", [128, 512], F32))

                t_b2 = self.dma("gpsimd", b2row[:], self.b2_in[None, :])
                ltw = []
                for i in range(HC):
                    ltw.append(self.dma("sync", W2[i][:], self.w2qT_s[ts128(i), :],
                                        deps=self.t_w0_dma))
                tr = P.op("vector", lambda v: v.tensor_scalar(
                    out=rs2v[:], in0=msqv[:], scalar1=1.0 / HID, scalar2=EPS_RMS,
                    op0=ALU.mult, op1=ALU.add), deps=[P.token("scalar"), P.token("vector")])
                tr = P.op("scalar", lambda sc: sc.activation(
                    out=rs2v[:], in_=rs2v[:], func=AF.Ln), deps=[tr])
                tr = P.op("scalar", lambda sc: sc.activation(
                    out=rs2v[:], in_=rs2v[:], func=AF.Exp, scale=-0.5),
                    deps=[tr], self_dep=True)
                tc = P.op("vector", lambda v: v.tensor_tensor(
                    out=c2v[:], in0=rs2v[:], in1=amaxv[:], op=ALU.mult), deps=[tr])
                tc = P.op("vector", lambda v: v.tensor_scalar(
                    out=c2v[:], in0=c2v[:], scalar1=EPS_Q, scalar2=None, op0=ALU.max),
                    deps=[tc], self_dep=True)
                ti = P.op("vector", lambda v: v.tensor_scalar(
                    out=c2v[:], in0=c2v[:], scalar1=1.0 / 127.0, scalar2=None,
                    op0=ALU.mult), deps=[tc], self_dep=True)
                ti = P.op("vector", lambda v: v.tensor_tensor(
                    out=c2v[:], in0=c2v[:], in1=self.sw2inv_bc[:].to_broadcast([128, NT]),
                    op=ALU.mult), deps=[ti, self.t_w0["vector"]], self_dep=True)
                ti = P.op("vector", lambda v: v.reciprocal(out=ic2v[:], in_=c2v[:]),
                          deps=[ti])
                ttr = P.op("tensor", lambda t_: t_.transpose(
                    ps_r[0:16, 0:128], ic2v[:, 0:NT], self.identf[:]),
                    deps=[ti, P.token("scalar")], drain=True)
                tst = P.op("scalar", lambda sc: sc.copy(out=st16b[:], in_=ps_r[0:16, 0:128]),
                           deps=[ttr])
                t_icrow = self.dma("sync", ic2row[0:1, :], st16b[:, :], deps=[tst])

                ev_bank = {0: None, 1: None}
                t_prev = [None, None]
                t_sdma = [None, None]
                for t in range(NT):
                    par = t % 2
                    lth = self.dma("sync", hqt[par][:], self.hq_t_s[t][:, :],
                                   deps=[t_prev[par]] + self.t_p3a_g)
                    ltx = self.dma("sync", xt[par][:], self.x_in[b, ts128(t), :],
                                   deps=[t_prev[par]])
                    lts = self.dma("sync", stl[par][:], self.s_s[b, ts128(t), :],
                                   deps=[t_prev[par]] + self.t_p2c_dma)
                    t_ev = []
                    for blk in range(2):
                        pp = blk
                        mm = None
                        for i in range(HC):
                            mm = P.op("tensor", lambda t_, i=i, blk=blk, par=par, pp=pp: t_.matmul(
                                ps_f[pp][:, :], hqt[par][:, ts128(i)],
                                W2[i][:, 512 * blk:512 * (blk + 1)],
                                start=(i == 0), stop=False),
                                deps=[lth, ltw[i], ev_bank[pp]])
                        mm = P.op("tensor", lambda t_, blk=blk, t=t, pp=pp: t_.matmul(
                            ps_f[pp][:, :], ic2row[0:1, ts128(t)],
                            b2row[0:1, 512 * blk:512 * (blk + 1)],
                            start=False, stop=True),
                            deps=[mm, t_b2, t_icrow], drain=True)
                        tv = P.op("scalar", lambda sc, blk=blk, t=t, pp=pp, b=b: sc.activation(
                            out=gt[:, 512 * blk:512 * (blk + 1)], in_=ps_f[pp][:, :],
                            func=AF.Silu, scale=c2v[:, t:t + 1]),
                            deps=[mm, ti])
                        ev_bank[pp] = tv
                        t_ev.append(tv)
                    t_gdump = t_sdump = None
                    if t == 0 and b == 0 and "p3_g0" in self.debug_names:
                        t_gdump = self.dump("p3_g0", gt[:], (128, D), F32, t_ev)
                        t_sdump = self.dump("p3_s0", stl[par][:], (128, D), BF16, [lts])
                        self.dump("p3_c2", c2v[:], (128, NT), F32, [ti])
                    tsg = P.op("vector", lambda v, par=par: v.tensor_tensor(
                        out=gt[:], in0=gt[:], in1=stl[par][:], op=ALU.mult),
                        deps=t_ev + [lts, t_gdump, t_sdump], self_dep=True)
                    tout = P.op("vector", lambda v, par=par: v.tensor_tensor(
                        out=ot[par][:], in0=gt[:], in1=xt[par][:], op=ALU.add),
                        deps=[tsg, ltx, t_sdma[par]], self_dep=True)
                    t_sdma[par] = self.dma("sync", self.out_hbm[b, ts128(t), :], ot[par][:],
                                           deps=[tout])
                    t_prev[par] = tout


def make_constants():
    ident = np.eye(128, dtype=ml_dtypes.bfloat16)
    identf = np.eye(128, dtype=np.float32)
    k = np.arange(128)[:, None]
    q = np.arange(128)[None, :]
    triT = np.where(k > q, np.float32(NEG), np.float32(0.0)).astype(ml_dtypes.bfloat16)
    tri01 = np.where(k > q, np.float32(0.0), np.float32(1.0)).astype(ml_dtypes.bfloat16)
    ew = np.zeros((128, 896), dtype=ml_dtypes.bfloat16)
    ew[np.arange(128), 384 + np.arange(128)] = 1.0
    return {"c_ident": ident, "c_identf": identf, "c_triT": triT, "c_tri01": tri01,
            "c_ew": ew}


def make_in_map(core, inputs):
    x = np.asarray(inputs["L0_attn_x"], np.float32)
    m = {
        "x": np.ascontiguousarray(x[BPC * core:BPC * (core + 1)]),
        "wq": np.asarray(inputs["L0_attn_Wq"], np.float32),
        "wk": np.asarray(inputs["L0_attn_Wk"], np.float32),
        "wv": np.asarray(inputs["L0_attn_Wv"], np.float32),
        "wo": np.asarray(inputs["L0_attn_Wo"], np.float32),
        "bq": np.asarray(inputs["L0_attn_bq"], np.float32),
        "bk": np.asarray(inputs["L0_attn_bk"], np.float32),
        "bv": np.asarray(inputs["L0_attn_bv"], np.float32),
        "bo": np.asarray(inputs["L0_attn_bo"], np.float32),
        "w1": np.asarray(inputs["L0_ffn_w1"], np.float32),
        "b1": np.asarray(inputs["L0_ffn_b1"], np.float32),
        "w2": np.asarray(inputs["L0_ffn_w2"], np.float32),
        "b2": np.asarray(inputs["L0_ffn_b2"], np.float32),
    }
    m.update(make_constants())
    return m


def kernel(**inputs):
    nc = Builder().build()
    in_maps = [make_in_map(c, inputs) for c in range(NCORES)]
    res = run_bass_kernel_spmd(nc, in_maps, core_ids=list(range(NCORES)))
    out = np.concatenate([r["out"] for r in res.results], axis=0)
    return out.reshape(16, N, D)
